# version 01 (final)
# baseline (speedup 1.0000x reference)
import numpy as np
import concourse.bass as bass
import concourse.mybir as mybir

F32 = mybir.dt.float32
BF16 = mybir.dt.bfloat16
FP8 = mybir.dt.float8e4
ALU = mybir.AluOpType
AF = mybir.ActivationFunctionType
AX = mybir.AxisListType

D = 2048
T = 1024
NT = 8
FC = 16
H = 16
DK = 128
CH = 64
NCH = 16
IF = 8192
IS = 4096
NCORE = 8

ENGINES = ("sync", "tensor", "vector", "scalar", "gpsimd")


class Sem:
    __slots__ = ("h", "n")

    def __init__(self, h):
        self.h = h
        self.n = 0


class KB:
    def __init__(self, nc, core_id, dbg=()):
        self.nc = nc
        self.core = core_id
        self.q = {e: [] for e in ENGINES}
        self._sems = {}
        self._exits = []
        self.dbg = set(dbg)
        self.dbgt = {}

    def sem(self, name):
        if name not in self._sems:
            cm = self.nc.semaphore(name)
            self._sems[name] = Sem(cm.__enter__())
            self._exits.append(cm)
        return self._sems[name]

    def wait(self, e, sem, tgt=None):
        t = sem.n if tgt is None else tgt
        if t > 0:
            self.q[e].append(lambda eng, h=sem.h, t=t: eng.wait_ge(h, t))

    ENG_SEM = {"tensor": "pe", "vector": "dv", "scalar": "ac", "gpsimd": "po"}

    def op(self, e, fn, sem=None, k=1):
        if sem is None and e in self.ENG_SEM:
            sem, k = self.sem(self.ENG_SEM[e]), 1
        if sem is not None:
            sem.n += k
            self.q[e].append(lambda eng, h=sem.h, k=k: fn(eng).then_inc(h, k))
            return sem.n
        self.q[e].append(lambda eng: fn(eng))
        return None

    def selfwait(self, e):
        self.wait(e, self.sem(self.ENG_SEM[e]))

    def drain(self, e):
        self.q[e].append(lambda eng: eng.drain())

    def emit(self, block):
        for e in ENGINES:
            sec = getattr(block, e)

            def body(eng, _e=e):
                for fn in self.q[_e]:
                    fn(eng)
            sec(body)

    def din(self, name, shape, dtype=F32):
        return self.nc.dram_tensor(name, list(shape), dtype, kind="ExternalInput").ap()

    def dout(self, name, shape, dtype=F32):
        return self.nc.dram_tensor(name, list(shape), dtype, kind="ExternalOutput").ap()

    def dscr(self, name, shape, dtype=F32):
        return self.nc.dram_tensor(name, list(shape), dtype).ap()

    def sb(self, name, shape, dtype=F32):
        return self.nc.alloc_sbuf_tensor(name, list(shape), dtype).ap()

    def ps(self, name, shape, dtype=F32):
        return self.nc.alloc_psum_tensor(name, list(shape), dtype).ap()

    def dbg_out(self, tag, shape, dtype=F32):
        if tag in self.dbg:
            self.dbgt[tag] = self.dout("dbg_" + tag, shape, dtype)

    def dump(self, tag, sb_ap, dram_view=None, waits=()):
        if tag not in self.dbg:
            return
        dram = self.dbgt[tag] if dram_view is None else dram_view
        wsc = self.sem("wsc")
        for s, t in waits:
            self.wait("scalar", s, t)
        self.op("scalar", lambda e, d=dram, s=sb_ap: e.dma_start(d, s), wsc, 16)


def swap_view(ap2d):
    P, N = ap2d.shape
    base = ap2d.ap
    pstep = base[0][0]
    assert base[-1][0] == 1
    return bass.AP(ap2d.tensor, ap2d.offset + 1, [[pstep, P], [2, N // 2], [-1, 2]])


def a32(ap):
    return ap.rearrange("p (b e) -> p b e", e=32)



def host_consts(core_id):
    out = {}
    out["identf"] = np.eye(128, dtype=np.float32)
    pos0 = (core_id % 4) * T
    d = DK
    inv_freq = (1.0 / (10000.0 ** (np.arange(0, d, 2, dtype=np.float32) / np.float32(d)))).astype(np.float32)
    pos = np.arange(pos0, pos0 + T, dtype=np.float32)
    sinu = pos[:, None] * inv_freq[None, :]
    sin, cos = np.sin(sinu).astype(np.float32), np.cos(sinu).astype(np.float32)
    ct = np.empty((T, d), np.float32)
    st = np.empty((T, d), np.float32)
    ct[:, 0::2] = cos
    ct[:, 1::2] = cos
    st[:, 0::2] = -sin
    st[:, 1::2] = sin
    out["rope_ct"] = np.ascontiguousarray(np.tile(ct, (1, H)))
    out["rope_st"] = np.ascontiguousarray(np.tile(st, (1, H)))
    out["onesf"] = np.ones((128, 512), np.float32)
    return out


def wT_rhs_tiles(w, width=512):
    dout, din = w.shape
    wt = np.ascontiguousarray(w.T)
    a = wt.reshape(din // 128, 128, dout // width, width)
    return np.ascontiguousarray(a.transpose(2, 0, 1, 3))


def bias_col(b):
    return np.ascontiguousarray(b.reshape(-1, 128).T)



def decl_l0(k):
    h = {}
    h["x_in"] = k.din("x_in", [T, D])
    h["identf"] = k.din("identf", [128, 128])
    h["onesf"] = k.din("onesf", [128, 512])
    h["wkT"] = k.din("wkT", [4, FC, 128, 512])
    h["wvT"] = k.din("wvT", [4, FC, 128, 512])
    h["bkv_rows"] = k.din("bkv_rows", [2, D])
    h["ident_b"] = k.sb("ident_b", [128, 128], BF16)
    h["ident_f"] = k.sb("ident_f", [128, 128], F32)
    h["ones_b"] = k.sb("ones_b", [128, 512], BF16)

    h["xt"] = [k.sb("xt0", [128, D]), None]
    h["xnt"] = [k.sb("xnt0", [128, D])]
    h["stats"] = k.sb("stats", [128, 10, NT])
    h["amax"] = k.sb("amax", [128, 64])
    h["scal2"] = k.sb("scal2", [128, 64])
    h["recip2"] = k.sb("recip2", [128, 64])
    h["rexp"] = k.sb("rexp", [128, D])
    h["sexp"] = k.sb("sexp", [128, D], BF16)
    h["q8t"] = None
    h["qbf"] = k.sb("qbf", [128, D], BF16)
    h["deqt"] = k.sb("deqt", [128, D], BF16)
    h["xn_f"] = k.sb("xn_f", [128, FC, T], BF16)
    h["xn_deq_f"] = k.sb("xn_deq_f", [128, FC, T], BF16)
    h["AR3"] = k.sb("AR3", [128, 70656], mybir.dt.uint8)
    h["AR4"] = k.sb("AR4", [128, 32768], mybir.dt.uint8)
    h["xbf"] = carve(h["AR4"], 4096, [D], BF16)
    h["kro"] = carve(h["AR4"], 8192, [512], F32)
    h["ka"] = carve(h["AR4"], 10240, [512], F32)
    h["xt"][1] = carve(h["AR4"], 16384, [D], F32)
    h["q8t"] = carve(h["AR4"], 12288, [D], FP8)


    h["pbA"] = k.ps("pbA", [128, 512])
    h["pbB"] = k.ps("pbB", [128, 512])
    h["pbC"] = k.ps("pbC", [128, 512])
    h["pbD"] = k.ps("pbD", [128, 512])
    h["pbT"] = k.ps("pbT", [128, 1024], BF16)
    h["pbT2"] = k.ps("pbT2", [128, 1024], BF16)
    h["kv_sp"] = k.dscr("kv_sp", [2, H, NT, 128, DK], BF16)
    h["cc_in"] = k.dscr("cc_in", [H * 128, DK + 1])
    h["cc_out"] = k.dscr("cc_out", [NCORE * H * 128, DK + 1])
    return h


def phase_consts(k, h):
    wg = k.sem("wg")
    k.op("gpsimd", lambda e: e.dma_start(h["ident_f"], h["identf"]), wg, 16)
    k.op("gpsimd", lambda e: e.dma_start(h["ident_b"], h["identf"]), wg, 16)
    k.op("gpsimd", lambda e: e.dma_start(h["ones_b"], h["onesf"]), wg, 16)
    k.op("gpsimd", lambda e: e.dma_start(h["bkv_b"][0:1], h["bkv_rows"].unsqueeze(0)), wg, 16)
    h["consts_wg"] = wg.n


def phase_ln_quant(k, h, x_src=None, xn_f_only=False):
    ac, dv, pe = k.sem("ac"), k.sem("dv"), k.sem("pe")
    sx = [k.sem("sx0"), k.sem("sx1")]
    st = h["stats"]
    bnb = h["bnbuf"]
    mv = h["mv"]
    x_src = h["x_in"] if x_src is None else x_src
    last_xt_reader_ac = [0] * NT
    last_xnt_reader_dv = [0] * NT
    last_xbf_reader_pe = [0] * NT
    last_deq_reader_pe = [0] * NT
    qbf_done_ac = 0
    ev = 0
    k.wait("tensor", k.sem("wg"), h.get("consts_wg", 0))
    for tt in range(NT):
        xt = h["xt"][tt % 2]
        xnt = h["xnt"][0]
        if tt >= 2:
            k.wait("sync", ac, last_xt_reader_ac[tt - 2])
        k.op("sync", lambda e, xt=xt, tt=tt: e.dma_start(
            xt, x_src.rearrange("(n p) d -> n p d", p=128)[tt]), sx[tt % 2], 16)
        k.wait("vector", sx[tt % 2])
        for i in range(4):
            k.op("vector", lambda e, xt=xt, i=i: e.bn_stats(bnb[:, i * 6:(i + 1) * 6], xt[:, i * 512:(i + 1) * 512]))
        k.selfwait("vector")
        k.op("vector", lambda e, tt=tt: e.bn_aggr(mv[:, :, tt], bnb[:]))
        k.selfwait("vector")
        k.op("vector", lambda e, tt=tt: e.tensor_scalar(st[:, 4, tt:tt + 1], mv[:, 1:2, tt], 1e-5, None, op0=ALU.add))
        k.wait("scalar", dv)
        k.op("scalar", lambda e, tt=tt: e.activation(st[:, 0, tt:tt + 1], st[:, 4, tt:tt + 1], AF.Ln))
        k.drain("scalar")
        k.selfwait("scalar")
        k.op("scalar", lambda e, tt=tt: e.activation(st[:, 1, tt:tt + 1], st[:, 0, tt:tt + 1], AF.Exp, scale=-0.5))
        k.wait("vector", ac)
        k.op("vector", lambda e, tt=tt: e.tensor_scalar(st[:, 2, tt:tt + 1], st[:, 1, tt:tt + 1], 1.0, None, op0=ALU.mult))
        k.selfwait("vector")
        k.drain("vector")
        k.op("vector", lambda e, tt=tt: e.scalar_tensor_tensor(st[:, 3, tt:tt + 1], mv[:, 0:1, tt], -1.0, st[:, 2, tt:tt + 1], op0=ALU.mult, op1=ALU.mult))
        k.wait("scalar", dv)
        if tt >= 1:
            k.wait("scalar", dv, last_xnt_reader_dv[tt - 1])
        if tt >= 2:
            k.wait("scalar", pe, last_xbf_reader_pe[tt - 2])
        k.op("scalar", lambda e, xt=xt, xnt=xnt, tt=tt: e.activation(
            xnt, xt, AF.Identity, scale=st[:, 2, tt:tt + 1], bias=st[:, 3, tt:tt + 1]))
        k.selfwait("scalar")
        last_xt_reader_ac[tt] = k.op("scalar", lambda e, xnt=xnt: e.activation(
            h["xbf"], xnt, AF.Copy, bias=0.0))
        xbf_ac = last_xt_reader_ac[tt]
        deq_dv = None
        if not xn_f_only:
            k.wait("vector", ac)
            if tt >= 1:
                k.wait("vector", ac, qbf_done_ac)
            if tt >= 2:
                k.wait("vector", pe, last_deq_reader_pe[tt - 2])
            k.op("vector", lambda e, xnt=xnt: e.tensor_reduce(h["amax"], a32(xnt), axis=AX.X, op=ALU.max, apply_absolute_value=True))
            k.selfwait("vector")
            k.op("vector", lambda e: e.tensor_scalar(h["scal2"], h["amax"], 2.0 / 448.0, 2e-30, op0=ALU.mult, op1=ALU.max))
            k.selfwait("vector")
            k.op("vector", lambda e: e.reciprocal(h["recip2"], h["scal2"]))
            k.selfwait("vector")
            k.op("vector", lambda e: e.tensor_copy(a32(h["rexp"]), h["recip2"].unsqueeze(-1).broadcast_to([128, 64, 32])))
            k.op("vector", lambda e: e.tensor_copy(a32(h["sexp"]), h["scal2"].unsqueeze(-1).broadcast_to([128, 64, 32])))
            k.selfwait("vector")
            last_xnt_reader_dv[tt] = k.op("vector", lambda e, xnt=xnt: e.tensor_tensor(h["q8t"], xnt, h["rexp"], op=ALU.mult))
            k.wait("scalar", dv)
            qbf_done_ac = k.op("scalar", lambda e: e.activation(h["qbf"], h["q8t"], AF.Copy, bias=0.0))
            k.wait("vector", ac)
            deq_dv = k.op("vector", lambda e: e.tensor_tensor(h["deqt"], h["qbf"], h["sexp"], op=ALU.mult))
        else:
            last_xnt_reader_dv[tt] = dv.n
        srcs = [("xbf", h["xbf"], h["xn_f"], xbf_ac, ac)]
        if not xn_f_only:
            srcs.append(("deq", h["deqt"], h["xn_deq_f"], deq_dv, dv))
        for nm, src, slab, rdy, rdy_sem in srcs:
            k.wait("tensor", rdy_sem, rdy)
            if ev:
                k.wait("tensor", dv, ev)
            pe_half = []
            for half in range(2):
                pb = h["pbT"] if half == 0 else h["pbT2"]
                for fc8 in range(8):
                    fc = half * 8 + fc8
                    val = k.op("tensor", lambda e, fc=fc, fc8=fc8, pb=pb, src=src: e.transpose(
                        pb[:, fc8 * 128:(fc8 + 1) * 128], src[:, fc * 128:(fc + 1) * 128], h["ident_b"]),
                        pe if fc8 == 7 else None)
                pe_half.append(val)
            for half in range(2):
                pb = h["pbT"] if half == 0 else h["pbT2"]
                k.wait("vector", pe, pe_half[half])
                ev = k.op("vector", lambda e, pb=pb, slab=slab, half=half, tt=tt: e.tensor_copy(
                    slab[:, half * 8:(half + 1) * 8, tt * 128:(tt + 1) * 128],
                    pb.rearrange("p (a b) -> p a b", b=128)), dv)
            if nm == "xbf":
                last_xbf_reader_pe[tt] = pe_half[-1]
            else:
                last_deq_reader_pe[tt] = pe_half[-1]
    h["lnq_ev"] = ev


def decl_l0b(k, h):
    h["ctT"] = k.din("ctT", [4, NT, 128, 512])
    h["stT"] = k.din("stT", [4, NT, 128, 512])
    h["wk_buf"] = [carve(h["AR3"], 16448, [FC, 512], BF16)] * 2
    h["wv_buf"] = [carve(h["AR3"], 32832, [FC, 512], BF16)] * 2
    h["ct_b"] = [carve(h["AR3"], 57408 + 2048 * i, [512], F32) for i in range(2)]
    h["st_b"] = [carve(h["AR3"], 61504 + 2048 * i, [512], F32) for i in range(2)]

    h["bnbuf"] = k.sb("bnbuf", [128, 24])
    h["mv"] = k.sb("mv", [128, 2, NT])
    h["pbE"] = k.ps("pbE", [128, 512])
    h["pbF"] = k.ps("pbF", [128, 512])
    h["totals"] = carve(h["AR3"], 0, [H, DK + 1], F32)
    h["kb2"] = carve(h["AR3"], 49216, [512], F32)
    h["kc"] = carve(h["AR3"], 51264, [512], F32)
    h["kfin"] = [carve(h["AR3"], 53312 + 1024 * i, [512], BF16) for i in range(2)]
    h["vfin"] = [carve(h["AR3"], 55360 + 1024 * i, [512], BF16) for i in range(2)]
    h["bkv_b"] = carve(h["AR3"], 8256, [2, D], BF16)
    h["wsc_b"] = k.sb("wsc_b", [128, 64], BF16)


def phase_kv_totals(k, h, collective=True):
    import os
    SKIP_ROPE = os.environ.get("SKIP_ROPE")
    SKIP_SPILL = os.environ.get("SKIP_SPILL")
    SKIP_TOTALS = os.environ.get("SKIP_TOTALS")
    SKIP_Z = os.environ.get("SKIP_Z")
    """k/v projections (token-major), rope+elu on k, spill to kv_sp, and
    per-head [S|Z] totals -> cc_in; then AllGather."""
    ac, dv, pe = k.sem("ac"), k.sem("dv"), k.sem("pe")
    swk, swv = k.sem("swk"), k.sem("swv")
    sct = [k.sem("sct0"), k.sem("sct1")]
    sst = [k.sem("sst0"), k.sem("sst1")]
    ssk = [k.sem("ssk0"), k.sem("ssk1")]
    ssv = [k.sem("ssv0"), k.sem("ssv1")]
    scc = k.sem("scc")
    tot_banks = [h["pbC"], h["pbD"], h["pbE"], h["pbF"]]
    kfin_free_until = [(0, None)] * 2
    vfin_free_until = [(0, None)] * 2
    xn_ready_dv = h["lnq_ev"]
    pbA_free_dv = 0
    wbuf_free_pe = [0, 0]
    for ds in range(4):
        wkb = h["wk_buf"][ds % 2]
        wvb = h["wv_buf"][ds % 2]
        if wbuf_free_pe[0]:
            k.wait("gpsimd", pe, wbuf_free_pe[0])
        wk_wg = k.op("gpsimd", lambda e, ds=ds, wkb=wkb: e.dma_start(
            wkb, h["wkT"][ds].rearrange("f p n -> p f n")), swk, 16)
        wv_wg = k.op("gpsimd", lambda e, ds=ds, wvb=wvb: e.dma_start(
            wvb, h["wvT"][ds].rearrange("f p n -> p f n")), swv, 16)
        first_mm_of_ds = True
        for tt in range(NT):
            ctb = h["ct_b"][tt % 2]
            stb = h["st_b"][tt % 2]
            k.wait("sync", dv)
            k.op("sync", lambda e, ds=ds, tt=tt, ctb=ctb: e.dma_start(ctb, h["ctT"][ds, tt]), sct[tt % 2], 16)
            k.op("sync", lambda e, ds=ds, tt=tt, stb=stb: e.dma_start(stb, h["stT"][ds, tt]), sst[tt % 2], 16)
            if first_mm_of_ds:
                k.wait("tensor", swk, wk_wg)
                k.wait("tensor", swv, wv_wg)
                k.wait("tensor", dv, xn_ready_dv)
                first_mm_of_ds = False
            if pbA_free_dv:
                k.wait("tensor", dv, pbA_free_dv)
            for fc in range(FC):
                k.op("tensor", lambda e, fc=fc, tt=tt, wkb=wkb: e.matmul(
                    h["pbA"][:, :], h["xn_f"][:, fc, tt * 128:(tt + 1) * 128], wkb[:, fc, :],
                    start=(fc == 0), stop=False))
            kmm = k.op("tensor", lambda e, ds=ds: e.matmul(
                h["pbA"][:, :], h["ones_b"][0:1, 0:128], h["bkv_b"][0:1, 0, ds * 512:(ds + 1) * 512],
                start=False, stop=True), pe)
            for fc in range(FC):
                k.op("tensor", lambda e, fc=fc, tt=tt, wvb=wvb: e.matmul(
                    h["pbB"][:, :], h["xn_f"][:, fc, tt * 128:(tt + 1) * 128], wvb[:, fc, :],
                    start=(fc == 0), stop=False))
            vmm = k.op("tensor", lambda e, ds=ds: e.matmul(
                h["pbB"][:, :], h["ones_b"][0:1, 0:128], h["bkv_b"][0:1, 1, ds * 512:(ds + 1) * 512],
                start=False, stop=True), pe)
            kfin = h["kfin"][tt % 2]
            vfin = h["vfin"][tt % 2]
            k.wait("vector", pe, kmm)
            k.wait("vector", sct[tt % 2])
            k.wait("vector", sst[tt % 2])
            pf, sf = kfin_free_until[tt % 2]
            k.wait("vector", pe, pf)
            if sf:
                k.wait("vector", ssk[tt % 2], sf[0])
            if SKIP_ROPE:
                pbA_free = k.op("vector", lambda e: e.tensor_copy(h["kro"], h["pbA"][:, :]))
                k.selfwait("vector")
                kfin_dv = k.op("vector", lambda e, kfin=kfin: e.tensor_copy(kfin, h["kro"]))
            else:
                k.op("vector", lambda e, ctb=ctb: e.tensor_tensor(h["kro"], h["pbA"][:, :], ctb, op=ALU.mult))
                pbA_free = k.op("vector", lambda e, stb=stb: e.tensor_tensor(h["ka"], swap_view(h["pbA"][:, :]), stb, op=ALU.mult))
                k.selfwait("vector")
                k.op("vector", lambda e: e.tensor_tensor(h["kro"], h["kro"], h["ka"], op=ALU.add))
                k.selfwait("vector")
                k.op("vector", lambda e: e.tensor_scalar(h["ka"], h["kro"], 0.0, None, op0=ALU.min))
                k.wait("scalar", dv)
                k.op("scalar", lambda e: e.activation(h["kb2"], h["ka"], AF.Exp))
                k.op("vector", lambda e: e.tensor_scalar(h["kc"], h["kro"], 0.0, None, op0=ALU.max))
                k.wait("vector", ac)
                k.selfwait("vector")
                kfin_dv = k.op("vector", lambda e, kfin=kfin: e.tensor_tensor(kfin, h["kc"], h["kb2"], op=ALU.add))
            k.wait("scalar", pe, vmm)
            pf, sf = vfin_free_until[tt % 2]
            k.wait("scalar", pe, pf)
            if sf:
                k.wait("scalar", ssv[tt % 2], sf[1])
            vfin_ac = k.op("scalar", lambda e, vfin=vfin: e.activation(vfin, h["pbB"][:, :], AF.Copy, bias=0.0), ac)
            k.wait("scalar", dv, kfin_dv)
            k.selfwait("scalar")
            if not SKIP_SPILL:
                k.op("scalar", lambda e, kfin=kfin, ds=ds, tt=tt: e.dma_start(
                    h["kv_sp"][0, 4 * ds:4 * ds + 4, tt].rearrange("h p d -> p h d"),
                    kfin.rearrange("p (h d) -> p h d", d=128)), ssk[tt % 2], 16)
                k.op("scalar", lambda e, vfin=vfin, ds=ds, tt=tt: e.dma_start(
                    h["kv_sp"][1, 4 * ds:4 * ds + 4, tt].rearrange("h p d -> p h d"),
                    vfin.rearrange("p (h d) -> p h d", d=128)), ssv[tt % 2], 16)
            sp_done = (ssk[tt % 2].n, ssv[tt % 2].n)
            k.wait("tensor", dv, kfin_dv)
            k.wait("tensor", ac, vfin_ac)
            last = None
            for hh in ([] if SKIP_TOTALS else range(4)):
                bank = tot_banks[hh]
                st_flag = (tt == 0)
                end_flag = (tt == NT - 1)
                last = k.op("tensor", lambda e, kfin=kfin, vfin=vfin, hh=hh, bank=bank, st_flag=st_flag, end_flag=end_flag: e.matmul(
                    bank[:, 0:128], kfin[:, hh * 128:(hh + 1) * 128], vfin[:, hh * 128:(hh + 1) * 128],
                    start=st_flag, stop=(end_flag and bool(SKIP_Z))), pe if (SKIP_Z and hh == 3) else None)
                if not SKIP_Z:
                    last = k.op("tensor", lambda e, kfin=kfin, hh=hh, bank=bank, end_flag=end_flag: e.matmul(
                        bank[:, 128:129], kfin[:, hh * 128:(hh + 1) * 128], h["ones_b"][:, 0:1],
                        start=False, stop=end_flag), pe if hh == 3 else None)
            if SKIP_TOTALS:
                last = pe.n
            kfin_free_until[tt % 2] = (last, None if SKIP_SPILL else sp_done)
            vfin_free_until[tt % 2] = (last, None if SKIP_SPILL else sp_done)
            pbA_free_dv = pbA_free
            if tt == NT - 1:
                wbuf_free_pe[0] = last
        k.wait("vector", pe)
        evs = None
        for hh in ([] if SKIP_TOTALS else range(4)):
            evs = k.op("vector", lambda e, hh=hh, ds=ds: e.tensor_copy(
                h["totals"][:, 4 * ds + hh, :], tot_banks[hh][:, 0:129]), dv)
        if evs:
            k.wait("tensor", dv, evs)
    k.wait("scalar", dv)
    k.op("scalar", lambda e: e.dma_start(
        h["cc_in"].rearrange("(h p) e -> p h e", p=128), h["totals"]), scc, 16)
    cc = k.sem("cc")
    if collective:
        k.wait("gpsimd", scc)
        k.op("gpsimd", lambda e: e.collective_compute(
            "AllGather", ALU.bypass, replica_groups=[list(range(NCORE))],
            ins=[h["cc_in"]], outs=[h["cc_out"]]), cc, 1)
    h["cc_done"] = cc.n


U8 = mybir.dt.uint8
DTSZ = {F32: 4, BF16: 2, FP8: 1, U8: 1}


def carve(arena, off_bytes, shape, dtype):
    n = 1
    for s in shape:
        n *= s
    sz = DTSZ[dtype]
    assert off_bytes % 32 == 0
    v = arena[:, off_bytes:off_bytes + n * sz].bitcast(dtype)
    if len(shape) == 1:
        return v
    r = v
    spec = "p (" + " ".join(f"a{i}" for i in range(len(shape))) + ") -> p " + " ".join(f"a{i}" for i in range(len(shape)))
    kw = {f"a{i}": shape[i] for i in range(len(shape))}
    return v.rearrange(spec, **kw)



def decl_l0_ffn(k, h):
    h["gate_w_t"] = k.din("gate_w_t", [64, 128, D])
    h["up_w_t"] = k.din("up_w_t", [64, 128, D])
    h["down_w_t"] = k.din("down_w_t", [16, 128, IF])
    h["d_sp"] = k.dscr("d_sp", [64, 128, D], BF16)
    h["xnf_sp"] = k.dscr("xnf_sp", [FC, 128, T], BF16)
    h["stg"] = [carve(h["AR4"], 2048 * i, [8, 128], BF16) for i in range(2)]
    h["hq"] = carve(h["AR3"], 0, [NT, IF], FP8)
    h["hscal"] = carve(h["AR3"], NT * IF, [NT, 256], BF16)
    h["g_ffn"] = carve(h["AR4"], 0, [NT, D], BF16)
    xnf_flat = h["xn_f"].rearrange("p a b -> p (a b)")
    h["wg_rhs"] = [xnf_flat[:, 0:8192].rearrange("p (a b) -> p a b", b=512),
                   xnf_flat[:, 8192:16384].rearrange("p (a b) -> p a b", b=512)]
    xnq_flat = h["xn_deq_f"].rearrange("p a b -> p (a b)")
    h["h_deqT"] = xnq_flat.rearrange("p (a b) -> p a b", b=256)
    h["wq_raw"] = h["xt"]
    h["dtile"] = [xnf_flat[:, 0:512], xnf_flat[:, 512:1024]]
    return h


def emit_quant2048(k, h, src_ap, out_fp8_ap, out_scal_bf_ap, nblk=64):
    dv, ac = k.sem("dv"), k.sem("ac")
    am, sc, rc = h["amax"][:, 0:nblk], h["scal2"][:, 0:nblk], h["recip2"][:, 0:nblk]
    k.wait("vector", ac)
    k.op("vector", lambda e: e.tensor_reduce(am, src_ap.rearrange("p (b e) -> p b e", e=32), axis=AX.X, op=ALU.max, apply_absolute_value=True))
    k.selfwait("vector")
    k.op("vector", lambda e: e.tensor_scalar(sc, am, 2.0 / 448.0, 2e-30, op0=ALU.mult, op1=ALU.max))
    k.selfwait("vector")
    k.op("vector", lambda e: e.reciprocal(rc, sc))
    k.op("vector", lambda e: e.tensor_copy(out_scal_bf_ap, sc))
    k.selfwait("vector")
    k.op("vector", lambda e: e.tensor_copy(a32(h["rexp"][:, 0:nblk * 32]), rc.unsqueeze(-1).broadcast_to([128, nblk, 32])))
    k.selfwait("vector")
    return k.op("vector", lambda e: e.tensor_tensor(out_fp8_ap, src_ap, h["rexp"][:, 0:nblk * 32], op=ALU.mult))


def emit_deq2048(k, h, q8_ap, scal_bf_ap, out_bf_ap, nblk=64, qdv=None):
    dv, ac = k.sem("dv"), k.sem("ac")
    k.wait("scalar", dv)
    k.selfwait("scalar")
    qc = k.op("scalar", lambda e: e.activation(h["qbf"][:, 0:nblk * 32], q8_ap, AF.Copy, bias=0.0))
    k.wait("vector", ac, qc)
    k.wait("vector", k.sem("pe"))
    k.op("vector", lambda e: e.tensor_copy(a32(h["sexp"][:, 0:nblk * 32]), scal_bf_ap.unsqueeze(-1).broadcast_to([128, nblk, 32])))
    k.selfwait("vector")
    return k.op("vector", lambda e: e.tensor_tensor(out_bf_ap, h["qbf"][:, 0:nblk * 32], h["sexp"][:, 0:nblk * 32], op=ALU.mult))


def emit_transpose16(k, h, src_bf_ap, dst_writes, rdy_sem, rdy, nfc=16):
    dv, pe = k.sem("dv"), k.sem("pe")
    k.wait("tensor", rdy_sem, rdy)
    k.wait("tensor", dv)
    pe_half = []
    for half in range((nfc + 7) // 8):
        pb = h["pbT"] if half % 2 == 0 else h["pbT2"]
        n8 = min(8, nfc - half * 8)
        for fc8 in range(n8):
            fc = half * 8 + fc8
            val = k.op("tensor", lambda e, fc=fc, fc8=fc8, pb=pb: e.transpose(
                pb[:, fc8 * 128:(fc8 + 1) * 128], src_bf_ap[:, fc * 128:(fc + 1) * 128], h["ident_b"]),
                None if fc8 < n8 - 1 else pe)
        pe_half.append((pb, val, n8))
    evs = []
    for pb, val, n8 in pe_half:
        k.wait("vector", pe, val)
        out_ap = dst_writes[len(evs)]
        evs.append(k.op("vector", lambda e, pb=pb, out_ap=out_ap, n8=n8: e.tensor_copy(
            out_ap, pb.rearrange("p (a b) -> p a b", b=128)[:, 0:n8, :])))
    return pe_half[-1][1], evs[-1]


def phase_ffn(k, h, a_end_pe):
    dv, ac, pe, po = k.sem("dv"), k.sem("ac"), k.sem("pe"), k.sem("po")
    scc = k.sem("scc")
    sxf = k.sem("sxf")
    swq = [k.sem("swq0"), k.sem("swq1")]
    ssg = [k.sem("ssg0"), k.sem("ssg1")]
    sdt = [k.sem("sdt0"), k.sem("sdt1")]
    k.wait("scalar", pe, a_end_pe)
    k.op("scalar", lambda e: e.dma_start(
        h["xnf_sp"].rearrange("f p t -> p f t"), h["xn_f"]), sxf, 16)
    xf_spilled = sxf.n

    wq_free = [0, 0]
    wg_free_pe = 0
    psA_free = (0, 0)
    hq_started = False
    for isl in range(16):
        tr_dv_last = 0
        for wi, wsrc in enumerate((h["gate_w_t"], h["up_w_t"])):
            rhsbuf = h["wg_rhs"][wi]
            for rb4 in range(4):
                rb = isl * 4 + rb4
                slot = (rb4 + wi * 4) % 2
                wq = h["wq_raw"][slot]
                k.wait("sync", dv, wq_free[slot])
                k.op("sync", lambda e, wsrc=wsrc, rb=rb, wq=wq: e.dma_start(wq, wsrc[rb]), swq[slot], 16)
                k.wait("vector", swq[slot])
                qdv = emit_quant2048(k, h, wq, h["q8t"], h["wsc_b"])
                wq_free[slot] = qdv
                ddv = emit_deq2048(k, h, h["q8t"], h["wsc_b"], h["deqt"], qdv=qdv)
                if isl == 0 and wi == 0 and rb4 == 0 and "dbg_q" in k.dbg:
                    k.wait("scalar", dv, ddv)
                    k.op("scalar", lambda e, wq=wq: e.dma_start(k.dbgt["dbg_q"], wq), k.sem("wdbg"), 16)
                    k.op("scalar", lambda e: e.dma_start(k.dbgt["dbg_d"], h["deqt"]), k.sem("wdbg"), 16)
                if isl == 0 and wi == 0 and rb4 == 0:
                    k.wait("vector", sxf, xf_spilled)
                if wg_free_pe:
                    k.wait("vector", pe, wg_free_pe)
                    wg_free_pe = 0
                dsts = [rhsbuf[:, half * 8:(half + 1) * 8, rb4 * 128:(rb4 + 1) * 128] for half in range(2)]
                _, tr_dv_last = emit_transpose16(k, h, h["deqt"], dsts, dv, ddv)
        if isl == 0 and "dbg_wg" in k.dbg:
            k.wait("scalar", dv, tr_dv_last)
            k.op("scalar", lambda e, hs=hs: e.dma_start(k.dbgt["dbg_wg"], h["wg_rhs"][0]), k.sem("wdbg"), 16)
        k.wait("tensor", dv, tr_dv_last)
        mm_last = 0
        for tt in range(NT):
            pa_ac, pa_dv = psA_free
            if pa_ac:
                k.wait("tensor", ac, pa_ac)
                k.wait("tensor", dv, pa_dv)
            for wi, bank in ((0, h["pbA"]), (1, h["pbB"])):
                for fc in range(FC):
                    mm_last = k.op("tensor", lambda e, fc=fc, tt=tt, wi=wi, bank=bank: e.matmul(
                        bank[:, :], h["xn_deq_f"][:, fc, tt * 128:(tt + 1) * 128], h["wg_rhs"][wi][:, fc, :],
                        start=(fc == 0), stop=(fc == FC - 1)))
            k.wait("scalar", pe, mm_last)
            sil_ac = k.op("scalar", lambda e: e.activation(h["kro"][:, 0:512], h["pbA"][:, :], AF.Silu))
            k.wait("vector", ac, sil_ac)
            if not hq_started:
                k.wait("vector", scc)
                for snm in ("ssk0", "ssk1", "ssv0", "ssv1"):
                    k.wait("vector", k.sem(snm))
                hq_started = True
            k.selfwait("vector")
            hm_dv = k.op("vector", lambda e: e.tensor_tensor(h["ka"][:, 0:512], h["kro"][:, 0:512], h["pbB"][:, :], op=ALU.mult))
            psA_free = (sil_ac, hm_dv)
            k.selfwait("vector")
            qdv = emit_quant2048(k, h, h["ka"][:, 0:512], h["hq"][:, tt, isl * 512:(isl + 1) * 512],
                                 h["hscal"][:, tt, isl * 16:(isl + 1) * 16], nblk=16)
        wg_free_pe = mm_last

    stg_free = [(0, 0), (0, 0)]
    for ob in range(16):
        for cc in range(4):
            slot = (ob * 4 + cc) % 2
            wq = h["wq_raw"][slot]
            k.wait("sync", dv, wq_free[slot])
            k.op("sync", lambda e, ob=ob, cc=cc, wq=wq: e.dma_start(
                wq, h["down_w_t"][ob, :, cc * 2048:(cc + 1) * 2048]), swq[slot], 16)
            k.wait("vector", swq[slot])
            qdv = emit_quant2048(k, h, wq, h["q8t"], h["wsc_b"])
            wq_free[slot] = qdv
            ddv = emit_deq2048(k, h, h["q8t"], h["wsc_b"], h["deqt"], qdv=qdv)
            for half in range(2):
                sslot = (ob * 8 + cc * 2 + half) % 2
                stg = h["stg"][sslot]
                if stg_free[sslot][1]:
                    k.wait("vector", ssg[sslot], stg_free[sslot][1])
                k.wait("tensor", dv, ddv)
                k.wait("tensor", dv)
                pe_l = None
                pb = h["pbT"] if half == 0 else h["pbT2"]
                for fc8 in range(8):
                    fc = half * 8 + fc8
                    pe_l = k.op("tensor", lambda e, fc=fc, fc8=fc8, pb=pb: e.transpose(
                        pb[:, fc8 * 128:(fc8 + 1) * 128], h["deqt"][:, fc * 128:(fc + 1) * 128], h["ident_b"]))
                k.wait("vector", pe, pe_l)
                ev = k.op("vector", lambda e, pb=pb, stg=stg: e.tensor_copy(
                    stg, pb.rearrange("p (a b) -> p a b", b=128)))
                k.wait("scalar", dv, ev)
                sg = k.op("scalar", lambda e, stg=stg, ob=ob, cc=cc, half=half: e.dma_start(
                    h["d_sp"][cc * 16 + half * 8:cc * 16 + half * 8 + 8, :, ob * 128:(ob + 1) * 128].rearrange("f p i -> p f i"),
                    stg), ssg[sslot], 16)
                stg_free[sslot] = (ev, sg)

    hdq_free_pe = 0
    dt_free = [0, 0]
    for tq in range(4):
        ev_last = 0
        for tt2 in range(2):
            tt = tq * 2 + tt2
            for cc in range(4):
                qdv2 = dv.n
                ddv = emit_deq2048(k, h, h["hq"][:, tt, cc * 2048:(cc + 1) * 2048],
                                   h["hscal"][:, tt, cc * 64:(cc + 1) * 64], h["deqt"], qdv=None)
                if hdq_free_pe:
                    k.wait("vector", pe, hdq_free_pe)
                    hdq_free_pe = 0
                dsts = [h["h_deqT"][:, cc * 16 + half * 8:cc * 16 + (half + 1) * 8, tt2 * 128:(tt2 + 1) * 128] for half in range(2)]
                _, ev_last = emit_transpose16(k, h, h["deqt"], dsts, dv, ddv)
        k.wait("tensor", dv, ev_last)
        if tq == 0:
            k.wait("sync", ssg[0])
            k.wait("sync", ssg[1])
        mm_last = 0
        gf_prev = (0, 0)
        for os_ in range(4):
            for ic in range(64):
                slot = ic % 2
                dt = h["dtile"][slot]
                k.wait("sync", pe, dt_free[slot])
                k.op("sync", lambda e, ic=ic, os_=os_, dt=dt: e.dma_start(
                    dt, h["d_sp"][ic, :, os_ * 512:(os_ + 1) * 512]), sdt[slot], 16)
                k.wait("tensor", sdt[slot])
                if ic < 2 and gf_prev[0]:
                    k.wait("tensor", ac, gf_prev[0])
                for tt2, bank in ((0, h["pbA"]), (1, h["pbB"])):
                    mm_last = k.op("tensor", lambda e, ic=ic, tt2=tt2, bank=bank, dt=dt: e.matmul(
                        bank[:, :], h["h_deqT"][:, ic, tt2 * 128:(tt2 + 1) * 128], dt,
                        start=(ic == 0), stop=(ic == 63)))
                dt_free[slot] = mm_last
            k.wait("scalar", pe, mm_last)
            a1 = k.op("scalar", lambda e, tq=tq, os_=os_: e.activation(
                h["g_ffn"][:, tq * 2, os_ * 512:(os_ + 1) * 512], h["pbA"][:, :], AF.Silu))
            a2 = k.op("scalar", lambda e, tq=tq, os_=os_: e.activation(
                h["g_ffn"][:, tq * 2 + 1, os_ * 512:(os_ + 1) * 512], h["pbB"][:, :], AF.Silu))
            gf_prev = (a2, 0)
        hdq_free_pe = mm_last
    h["ffn_done_ac"] = ac.n



def decl_l0_attn(k, h):
    h["wqT_stat"] = k.din("wqT_stat", [16, FC, 128, 128])
    h["wgT_stat"] = k.din("wgT_stat", [16, FC, 128, 128])
    h["woT"] = k.din("woT", [4, FC, 128, 512])
    h["bqg_cols"] = k.din("bqg_cols", [128, 32])
    h["bo_row"] = k.din("bo_row", [1, D])
    h["rope_cf"] = k.din("rope_cf", [128, T])
    h["rope_sf"] = k.din("rope_sf", [128, T])
    h["pswap"] = k.din("pswap", [128, 128])
    h["wmask"] = k.din("wmask", [128, 8])
    h["x1_sp"] = k.dscr("x1_sp", [T, D])
    A3 = h["AR3"]
    h["bqg_b"] = carve(A3, 20480, [32], F32)
    h["cf"] = carve(A3, 12288, [T], F32)
    h["sf"] = carve(A3, 16384, [T], F32)
    h["pswap_b"] = carve(A3, 20736, [128], BF16)
    h["wmask_b"] = carve(A3, 21120, [8], F32)
    h["S_state"] = carve(A3, 0, [H, DK + 1], F32)
    h["S_bf"] = carve(A3, 8256 + 32, [DK + 1], BF16)
    h["den_s"] = carve(A3, 8864, [64], F32)
    h["dinv_b"] = carve(A3, 9184, [64], BF16)
    h["denB_s"] = carve(A3, 9504, [2, 64], F32)
    h["o1_s"] = carve(A3, 10016, [2, 64], F32)
    h["cstage"] = carve(A3, 24768, [H, DK + 1], F32)
    h["kh_sb"] = [carve(A3, 24768, [NCH, 128], BF16), carve(A3, 28864, [NCH, 128], BF16)]
    h["vh_sb"] = [carve(A3, 32960, [NCH, 128], BF16), carve(A3, 37056, [NCH, 128], BF16)]
    h["qsc"] = carve(A3, 41152, [T], F32)
    h["q1"] = carve(A3, 45248, [T], F32)
    h["q2"] = carve(A3, 49344, [T], F32)
    h["qh_b"] = [carve(A3, 53440, [T], BF16), carve(A3, 55488, [T], BF16)]
    h["gh_b"] = [carve(A3, 57536, [T], BF16), carve(A3, 59584, [T], BF16)]
    h["wqg_buf"] = [carve(A3, 61632, [FC, 128], BF16), carve(A3, 65728, [FC, 128], BF16)]
    h["wo_buf"] = carve(A3, 24768, [FC, 512], BF16)
    h["bo_b"] = carve(A3, 61632, [1, D], BF16)
    h["att"] = h["xn_deq_f"]
    h["sres"] = h["rexp"]
    h["xtD"] = [None, carve(h["AR3"], 41152, [D], F32)]


def phase_attn(k, h, b_end):
    dv, ac, pe, po = k.sem("dv"), k.sem("ac"), k.sem("pe"), k.sem("po")
    cc = k.sem("cc")
    sxr = k.sem("sxr")
    swg2 = [k.sem("swg2_0"), k.sem("swg2_1")]
    skh = [k.sem("skh0"), k.sem("skh1")]
    scst = k.sem("scst")
    sx1 = k.sem("sx1")
    
    k.wait("sync", pe, b_end["pe"])
    k.wait("sync", k.sem("sdt0"))
    k.wait("sync", k.sem("sdt1"))
    k.op("sync", lambda e: e.dma_start(h["xn_f"], h["xnf_sp"].rearrange("f p t -> p f t")), sxr, 16)
    xnf_rdy = sxr.n
    k.op("sync", lambda e: e.dma_start(h["bqg_b"], h["bqg_cols"]), sxr, 16)
    k.op("sync", lambda e: e.dma_start(h["cf"], h["rope_cf"]), sxr, 16)
    k.op("sync", lambda e: e.dma_start(h["sf"], h["rope_sf"]), sxr, 16)
    k.wait("gpsimd", k.sem("dv"), b_end["dv"])
    k.wait("gpsimd", k.sem("ac"), b_end["ac"])
    k.wait("gpsimd", k.sem("pe"), b_end["pe"])
    k.op("gpsimd", lambda e: e.dma_start(h["pswap_b"], h["pswap"]), k.sem("spsw"), 16)
    consts_rdy = (sxr.n, k.sem("spsw").n)
    k.wait("vector", ac, b_end["ac"])
    k.wait("vector", dv)
    k.op("vector", lambda e: e.memset(h["S_state"].rearrange("p a b -> p (a b)"), 0.0))
    ms_dv = dv.n
    k.op("sync", lambda e: e.dma_start(h["wmask_b"], h["wmask"]), scst, 16)
    k.wait("sync", cc, h["cc_done"])
    for cid in range(NCORE):
        k.wait("sync", dv, ms_dv if cid == 0 else dv.n)
        k.op("sync", lambda e, cid=cid: e.dma_start(
            h["cstage"].rearrange("p a b -> p (a b)").rearrange("p (a b) -> p a b", b=DK + 1),
            h["cc_out"][cid * H * 128:(cid + 1) * H * 128, :].rearrange("(a p) b -> p a b", p=128)), scst, 16)
        k.wait("vector", cc, h["cc_done"])
        k.wait("vector", scst)
        k.selfwait("vector")
        k.op("vector", lambda e, cid=cid: e.scalar_tensor_tensor(
            h["S_state"].rearrange("p a b -> p (a b)"), h["cstage"].rearrange("p a b -> p (a b)"),
            h["wmask_b"][:, cid:cid + 1], h["S_state"].rearrange("p a b -> p (a b)"),
            op0=ALU.mult, op1=ALU.add))
    sbase_dv = dv.n
    k.selfwait("vector")
    for b in range(2):
        k.op("vector", lambda e, b=b: e.memset(h["kh_sb"][b].rearrange("p a b -> p (a b)")[64:128, :], 0.0))
        k.op("vector", lambda e, b=b: e.memset(h["vh_sb"][b].rearrange("p a b -> p (a b)")[64:128, :], 0.0))
    pad_dv = dv.n
    k.wait("sync", dv, sbase_dv)

    qg_free = [(0, 0)] * 2
    kv_free = [0, 0]
    wqg_free = [0, 0]
    att_written_dv = 0
    for hh in range(H):
        slot = hh % 2
        if wqg_free[slot]:
            k.wait("gpsimd", pe, wqg_free[slot])
        wql = k.op("gpsimd", lambda e, hh=hh, slot=slot: e.dma_start(
            h["wqg_buf"][slot], h["wqT_stat"][hh].rearrange("f p n -> p f n")), swg2[slot], 16)
        if kv_free[slot]:
            k.wait("sync", pe, kv_free[slot])
        k.op("sync", lambda e, hh=hh, slot=slot: e.dma_start(
            h["kh_sb"][slot][0:64, :, :],
            h["kv_sp"][0, hh].rearrange("t (h2 r) d -> r (t h2) d", h2=2)), skh[slot], 16)
        k.op("sync", lambda e, hh=hh, slot=slot: e.dma_start(
            h["vh_sb"][slot][0:64, :, :],
            h["kv_sp"][1, hh].rearrange("t (h2 r) d -> r (t h2) d", h2=2)), skh[slot], 16)
        kh_rdy = skh[slot].n
        k.wait("tensor", swg2[slot], wql)
        if hh == 0:
            k.wait("tensor", sxr, consts_rdy[0])
            k.wait("tensor", k.sem("spsw"), consts_rdy[1])
            k.wait("tensor", dv, pad_dv)
        qmm = []
        for half, bank in ((0, h["pbA"]), (1, h["pbB"])):
            for fc in range(FC):
                m = k.op("tensor", lambda e, fc=fc, half=half, bank=bank, slot=slot: e.matmul(
                    bank[:, :], h["wqg_buf"][slot][:, fc, :], h["xn_f"][:, fc, half * 512:(half + 1) * 512],
                    start=(fc == 0), stop=(fc == FC - 1)))
            qmm.append(m)
        k.wait("scalar", pe, qmm[0])
        k.op("scalar", lambda e, hh=hh: e.activation(h["qsc"][:, 0:512], h["pbA"][:, :], AF.Identity, bias=h["bqg_b"][:, hh:hh + 1]))
        k.wait("scalar", pe, qmm[1])
        k.op("scalar", lambda e, hh=hh: e.activation(h["qsc"][:, 512:1024], h["pbB"][:, :], AF.Identity, bias=h["bqg_b"][:, hh:hh + 1]))
        k.selfwait("scalar")
        qbf_ac = k.op("scalar", lambda e, slot=slot: e.activation(h["qh_b"][slot], h["qsc"], AF.Copy, bias=0.0))
        k.wait("tensor", ac, qbf_ac)
        sw = []
        for half, bank in ((0, h["pbA"]), (1, h["pbB"])):
            m = k.op("tensor", lambda e, half=half, bank=bank, slot=slot: e.matmul(
                bank[:, :], h["pswap_b"], h["qh_b"][slot][:, half * 512:(half + 1) * 512],
                start=True, stop=True))
            sw.append(m)
        k.wait("vector", ac, qbf_ac)
        k.op("vector", lambda e: e.tensor_tensor(h["q1"], h["qsc"], h["cf"], op=ALU.mult))
        k.wait("vector", pe, sw[0])
        k.op("vector", lambda e: e.tensor_tensor(h["q2"][:, 0:512], h["pbA"][:, :], h["sf"][:, 0:512], op=ALU.mult))
        k.wait("vector", pe, sw[1])
        k.op("vector", lambda e: e.tensor_tensor(h["q2"][:, 512:1024], h["pbB"][:, :], h["sf"][:, 512:1024], op=ALU.mult))
        k.selfwait("vector")
        k.op("vector", lambda e: e.tensor_tensor(h["q1"], h["q1"], h["q2"], op=ALU.add))
        k.selfwait("vector")
        k.op("vector", lambda e: e.tensor_scalar(h["q2"], h["q1"], 0.0, None, op0=ALU.min))
        qm_dv = dv.n
        k.wait("scalar", dv, qm_dv)
        k.selfwait("scalar")
        ex_ac = k.op("scalar", lambda e: e.activation(h["q2"], h["q2"], AF.Exp))
        k.op("vector", lambda e: e.tensor_scalar(h["q1"], h["q1"], 0.0, None, op0=ALU.max))
        k.wait("vector", ac, ex_ac)
        k.selfwait("vector")
        qh_dv = k.op("vector", lambda e, slot=slot: e.tensor_tensor(h["qh_b"][slot], h["q1"], h["q2"], op=ALU.add))
        k.wait("gpsimd", pe, qmm[-1])
        wgl = k.op("gpsimd", lambda e, hh=hh, slot=slot: e.dma_start(
            h["wqg_buf"][slot], h["wgT_stat"][hh].rearrange("f p n -> p f n"), ), swg2[slot], 16)
        k.wait("tensor", swg2[slot], wgl)
        k.wait("tensor", dv, qm_dv)
        gmm = []
        for half, bank in ((0, h["pbA"]), (1, h["pbB"])):
            for fc in range(FC):
                m = k.op("tensor", lambda e, fc=fc, half=half, bank=bank, slot=slot: e.matmul(
                    bank[:, :], h["wqg_buf"][slot][:, fc, :], h["xn_f"][:, fc, half * 512:(half + 1) * 512],
                    start=(fc == 0), stop=(fc == FC - 1)))
            gmm.append(m)
        wqg_free[slot] = gmm[-1]
        k.wait("scalar", pe, gmm[0])
        k.op("scalar", lambda e, hh=hh, slot=slot: e.activation(h["gh_b"][slot][:, 0:512], h["pbA"][:, :], AF.Sigmoid, bias=h["bqg_b"][:, 16 + hh:17 + hh]))
        k.wait("scalar", pe, gmm[1])
        gh_ac = k.op("scalar", lambda e, hh=hh, slot=slot: e.activation(h["gh_b"][slot][:, 512:1024], h["pbB"][:, :], AF.Sigmoid, bias=h["bqg_b"][:, 16 + hh:17 + hh]))
        k.wait("tensor", skh[slot], kh_rdy)
        k.wait("tensor", dv, qh_dv)
        k.wait("tensor", ac, gh_ac)
        if hh == 0:
            k.wait("vector", dv, sbase_dv)
        S = h["S_state"][:, hh, :]
        last_epi_dv = 0
        for c in range(NCH):
            if c > 0 or hh > 0:
                k.wait("tensor", dv)
            u1 = k.op("tensor", lambda e, c=c, slot=slot: e.matmul(
                h["pbC"][:, 0:128], h["kh_sb"][slot][:, c, :], h["vh_sb"][slot][:, c, :], start=True, stop=False))
            u2 = k.op("tensor", lambda e, c=c, slot=slot: e.matmul(
                h["pbC"][:, 128:129], h["kh_sb"][slot][:, c, :], h["ones_b"][:, 0:1], start=False, stop=True))
            k.wait("vector", pe, u2)
            k.selfwait("vector")
            sadd = k.op("vector", lambda e, S=S: e.tensor_tensor(S, S, h["pbC"][:, 0:129], op=ALU.add))
            k.wait("scalar", dv, sadd)
            k.selfwait("scalar")
            sbf = k.op("scalar", lambda e, S=S: e.activation(h["S_bf"], S, AF.Copy, bias=0.0))
            k.wait("tensor", ac, sbf)
            nm = k.op("tensor", lambda e, c=c, slot=slot: e.matmul(
                h["pbD"][:, 0:64], h["S_bf"][:, 0:128], h["qh_b"][slot][:, c * 64:(c + 1) * 64], start=True, stop=True))
            dn = k.op("tensor", lambda e, c=c, slot=slot: e.matmul(
                h["pbE"][0:1, 0:64], h["S_bf"][:, 128:129], h["qh_b"][slot][:, c * 64:(c + 1) * 64], start=True, stop=True))
            k.wait("vector", pe, dn)
            k.selfwait("vector")
            k.op("vector", lambda e: e.tensor_scalar(h["den_s"][0:1, :], h["pbE"][0:1, 0:64], 1e-6, None, op0=ALU.add))
            k.selfwait("vector")
            def _recip(e):
                with k.nc.allow_low_precision(reason="denominator recip to bf16 (bf16-matmul-class noise)"):
                    return e.reciprocal(h["dinv_b"][0:1, :], h["den_s"][0:1, :])
            di = k.op("vector", _recip)
            k.wait("tensor", dv, di)
            db = k.op("tensor", lambda e: e.matmul(
                h["pbF"][:, 0:64], h["ones_b"][0:1, 0:128], h["dinv_b"][0:1, :], start=True, stop=True))
            k.wait("vector", pe, db)
            k.selfwait("vector")
            k.op("vector", lambda e: e.tensor_copy(h["denB_s"][:, 0, :], h["pbF"][:, 0:64]))
            k.selfwait("vector")
            k.op("vector", lambda e: e.tensor_tensor(h["o1_s"][:, 0, :], h["pbD"][:, 0:64], h["denB_s"][:, 0, :], op=ALU.mult))
            k.selfwait("vector")
            last_epi_dv = k.op("vector", lambda e, hh=hh, c=c, slot=slot: e.tensor_tensor(
                h["att"][:, hh, c * 64:(c + 1) * 64], h["o1_s"][:, 0, :], h["gh_b"][slot][:, c * 64:(c + 1) * 64], op=ALU.mult))
        kv_free[slot] = pe.n
        att_written_dv = last_epi_dv
    h["attn_done"] = dict(dv=att_written_dv, pe=k.sem("pe").n, ac=k.sem("ac").n, dvf=k.sem("dv").n)


def phase_wo_epilogue(k, h):
    dv, ac, pe = k.sem("dv"), k.sem("ac"), k.sem("pe")
    swo = k.sem("swo")
    sxt = [k.sem("sxt0"), k.sem("sxt1")]
    sx1 = k.sem("sx1")
    sbo = k.sem("sbo")
    attn_pe = h["attn_done"]["pe"]
    k.wait("gpsimd", pe, attn_pe)
    k.wait("sync", ac, h["attn_done"]["ac"])
    k.wait("sync", dv, h["attn_done"]["dvf"])
    k.op("gpsimd", lambda e: e.dma_start(h["bo_b"][0:1], h["bo_row"].unsqueeze(1)), sbo, 16)
    att_dv = h["attn_done"]["dv"]
    wo_free_pe = attn_pe
    xt_free = [0, 0]
    bank_free = [0, 0]
    for ds in range(4):
        if wo_free_pe:
            k.wait("gpsimd", pe, wo_free_pe)
        wol = k.op("gpsimd", lambda e, ds=ds: e.dma_start(
            h["wo_buf"], h["woT"][ds].rearrange("f p n -> p f n")), swo, 16)
        k.wait("tensor", swo, wol)
        if ds == 0:
            k.wait("tensor", dv, att_dv)
            k.wait("tensor", sbo)
        for tt in range(NT):
            slot = tt % 2
            xbuf = h["xt"][0] if slot == 0 else h["xtD"][1]
            if xt_free[slot]:
                k.wait("sync", dv, xt_free[slot])
            k.op("sync", lambda e, tt=tt, xbuf=xbuf: e.dma_start(
                xbuf, h["x_in"].rearrange("(n p) d -> n p d", p=128)[tt]), sxt[slot], 16)
            bank = h["pbA"] if slot == 0 else h["pbB"]
            if bank_free[slot]:
                k.wait("tensor", dv, bank_free[slot])
            mm = 0
            for ec in range(FC):
                mm = k.op("tensor", lambda e, ec=ec, tt=tt, bank=bank: e.matmul(
                    bank[:, :], h["att"][:, ec, tt * 128:(tt + 1) * 128],
                    h["wo_buf"][:, ec, :], start=(ec == 0), stop=False))
            mm = k.op("tensor", lambda e, ds=ds, bank=bank: e.matmul(
                bank[:, :], h["ones_b"][0:1, 0:128],
                h["bo_b"][0:1, 0, ds * 512:(ds + 1) * 512], start=False, stop=True))
            k.wait("vector", pe, mm)
            k.wait("vector", sxt[slot])
            k.wait("vector", sx1)
            k.selfwait("vector")
            k.op("vector", lambda e, bank=bank, tt=tt, ds=ds: e.tensor_tensor(
                h["sres"][:, 0:512], bank[:, :], h["g_ffn"][:, tt, ds * 512:(ds + 1) * 512], op=ALU.mult))
            k.selfwait("vector")
            pr_dv = k.op("vector", lambda e, tt=tt, ds=ds, xbuf=xbuf: e.tensor_tensor(
                h["xnt"][0][:, ds * 512:(ds + 1) * 512],
                xbuf[:, ds * 512:(ds + 1) * 512], h["sres"][:, 0:512], op=ALU.add))
            xt_free[slot] = pr_dv
            bank_free[slot] = pr_dv
            k.wait("scalar", dv, pr_dv)
            k.op("scalar", lambda e, tt=tt, ds=ds: e.dma_start(
                h["x1_sp"].rearrange("(n p) d -> n p d", p=128)[tt, :, ds * 512:(ds + 1) * 512],
                h["xnt"][0][:, ds * 512:(ds + 1) * 512]), sx1, 16)
        wo_free_pe = pe.n
    h["l0_done"] = dict(sx1=sx1.n)


IC2 = 32


def decl_l1(k, h, standalone=False):
    if standalone:
        h["x1_sp"] = k.din("x1_in", [T, D])
    h["winT_stat"] = k.din("winT_stat", [64, FC, 128, 128])
    h["convT_stat"] = k.din("convT_stat", [IC2, IC2, 128, 128])
    h["woutT_stat"] = k.din("woutT_stat", [FC, IC2, 128, 128])
    h["dcol"] = k.din("dcol", [128, IC2])
    h["gwT"] = k.din("gwT", [FC, 128, 8])
    h["ebT8"] = k.din("ebT8", [8, FC, 128])
    h["esel"] = k.din("esel", [8, 8, 128])
    h["moeT_stat"] = k.din("moeT_stat", [8, FC, FC, 128, 128])
    h["x_out"] = k.dout("x_out", [T, D])
    A3, A4 = h["AR3"], h["AR4"]
    h["xn2_f"] = h["xn_f"]
    h["xp"] = h["xn_deq_f"].rearrange("p a b -> p (a b)").rearrange("p (a b) -> p a b", b=512)
    h["sz"] = carve(A4, 0, [IC2, 512], BF16)
    h["s2"] = carve(A3, 0, [FC, T], BF16)
    h["mo_f"] = carve(A4, 0, [FC, 512], BF16)
    h["xe"] = carve(A3, 32768, [FC, 512], BF16)
    h["prod"] = carve(A3, 49408, [FC, 512], BF16)
    h["GT2"] = carve(A3, 66560, [T], BF16)[0:8]
    h["wl_g"] = [carve(A3, 32768 + 8192 * i, [IC2, 128], BF16) for i in range(2)]
    h["wl_h"] = [carve(A4, 16384 + 8192 * i, [IC2, 128], BF16) for i in range(2)]
    h["gw_b"] = h["qbf"][:, 0:128].rearrange("p (a b) -> p a b", b=8)
    h["ebT_b"] = h["deqt"].rearrange("p (a b) -> p a b", b=128)[0:8]
    h["esel_b"] = h["sexp"][:, 0:1024].rearrange("p (a b) -> p a b", b=128)[0:8]
    h["gtile"] = h["rexp"][:, 0:64].rearrange("p (a b) -> p a b", b=8)
    h["gscr"] = h["rexp"][:, 64:96].rearrange("p (a b) -> p a b", b=8)
    h["gmsk"] = h["rexp"][:, 96:120].rearrange("p (a b) -> p a b", b=8)
    h["cvt"] = h["rexp"][:, 128:640]
    h["dcol_b"] = h["rexp"][:, 704:736]
    h["sil_scr"] = h["qbf"][:, 1024:1536]
    h["x1t"] = [h["xt"][0], h["xt"][0]]
    h["prcp"] = carve(A3, 32768, [D], BF16)


def phase_barrier(k, extra_sems=()):
    for e in ("sync", "tensor", "vector", "scalar", "gpsimd"):
        for s in ("pe", "dv", "ac"):
            k.wait(e, k.sem(s))
        for s in extra_sems:
            k.wait(e, k.sem(s))


def phase_l1_attn(k, h):
    dv, ac, pe, po = k.sem("dv"), k.sem("ac"), k.sem("dv"), k.sem("po")
    pe = k.sem("pe")
    swl = [k.sem("swl0"), k.sem("swl1")]
    sdc = k.sem("sdc")
    k.wait("sync", k.sem("dv"))
    k.op("sync", lambda e: e.dma_start(h["dcol_b"], h["dcol"]), sdc, 16)
    k.wait("vector", sdc)
    k.wait("tensor", k.sem("dv"))
    wl_free = [0, 0]
    for half in range(2):
        hs = slice(half * 512, (half + 1) * 512)
        for ic in range(64):
            slot = ic % 2
            if wl_free[slot]:
                k.wait("gpsimd", pe, wl_free[slot])
            wld = k.op("gpsimd", lambda e, ic=ic, slot=slot: e.dma_start(
                h["wl_g"][slot][:, 0:FC, :], h["winT_stat"][ic].rearrange("f p n -> p f n")), swl[slot], 16)
            k.wait("tensor", swl[slot], wld)
            bank = h["pbA"] if slot == 0 else h["pbB"]
            mm = 0
            for fc in range(FC):
                mm = k.op("tensor", lambda e, fc=fc, ic=ic, half=half, bank=bank, slot=slot, hs=hs: e.matmul(
                    bank[:, :], h["wl_g"][slot][:, fc, :], h["xn2_f"][:, fc, hs],
                    start=(fc == 0), stop=(fc == FC - 1)))
            wl_free[slot] = mm
            if ic < IC2:
                k.wait("vector", pe, mm)
                k.selfwait("vector")
                k.op("vector", lambda e, ic=ic, bank=bank: e.tensor_copy(h["xp"][:, ic, :], bank[:, :]))
                bank_free = dv.n
                k.wait("tensor", dv, bank_free)
            else:
                k.wait("scalar", pe, mm)
                k.selfwait("scalar")
                k.op("scalar", lambda e, ic=ic, bank=bank: e.activation(h["sz"][:, ic - IC2, :], bank[:, :], AF.Silu))
                k.wait("tensor", ac)
        xp_dv = dv.n
        sz_ac = ac.n
        for i2 in range(IC2):
            slot = i2 % 2
            if wl_free[slot]:
                k.wait("gpsimd", pe, wl_free[slot])
            wld = k.op("gpsimd", lambda e, i2=i2, slot=slot: e.dma_start(
                h["wl_g"][slot], h["convT_stat"][i2].rearrange("f p n -> p f n")), swl[slot], 16)
            k.wait("tensor", swl[slot], wld)
            if i2 == 0:
                k.wait("tensor", dv, xp_dv)
            bank = h["pbA"] if slot == 0 else h["pbB"]
            mm = 0
            for ic in range(IC2):
                mm = k.op("tensor", lambda e, ic=ic, i2=i2, bank=bank, slot=slot: e.matmul(
                    bank[:, :], h["wl_g"][slot][:, ic, :], h["xp"][:, ic, :],
                    start=(ic == 0), stop=(ic == IC2 - 1)))
            wl_free[slot] = mm
            k.wait("vector", pe, mm)
            if i2 == 0:
                k.wait("vector", ac, sz_ac)
            k.selfwait("vector")
            k.op("vector", lambda e, i2=i2, bank=bank: e.tensor_scalar(
                h["cvt"], bank[:, :], h["dcol_b"][:, i2:i2 + 1], None, op0=ALU.mult))
            k.selfwait("vector")
            k.op("vector", lambda e, i2=i2: e.tensor_tensor(h["sz"][:, i2, :], h["cvt"], h["sz"][:, i2, :], op=ALU.mult))
            k.wait("tensor", dv)
        yz_dv = dv.n
        for oc in range(FC):
            slot = oc % 2
            if wl_free[slot]:
                k.wait("gpsimd", pe, wl_free[slot])
            wld = k.op("gpsimd", lambda e, oc=oc, slot=slot: e.dma_start(
                h["wl_g"][slot], h["woutT_stat"][oc].rearrange("f p n -> p f n")), swl[slot], 16)
            k.wait("tensor", swl[slot], wld)
            if oc == 0:
                k.wait("tensor", dv, yz_dv)
            bank = h["pbA"] if slot == 0 else h["pbB"]
            mm = 0
            for ic in range(IC2):
                mm = k.op("tensor", lambda e, ic=ic, oc=oc, bank=bank, slot=slot: e.matmul(
                    bank[:, :], h["wl_g"][slot][:, ic, :], h["sz"][:, ic, :],
                    start=(ic == 0), stop=(ic == IC2 - 1)))
            wl_free[slot] = mm
            k.wait("scalar", pe, mm)
            k.selfwait("scalar")
            k.op("scalar", lambda e, oc=oc, half=half, bank=bank, hs=hs: e.activation(h["s2"][:, oc, hs], bank[:, :], AF.Copy, bias=0.0))
            k.wait("tensor", ac)
    h["g_done"] = dict(pe=pe.n, dv=dv.n, ac=ac.n)


def phase_l1_moe(k, h):
    dv, ac, pe = k.sem("dv"), k.sem("ac"), k.sem("pe")
    swl = [k.sem("swl0"), k.sem("swl1")]
    sgw = k.sem("sgw")
    sx1l = [k.sem("sx1l0"), k.sem("sx1l1")]
    sout = k.sem("sout")
    k.wait("gpsimd", pe)
    k.op("gpsimd", lambda e: e.dma_start(h["gw_b"], h["gwT"].rearrange("f p n -> p f n")), sgw, 16)
    k.op("gpsimd", lambda e: e.dma_start(h["ebT_b"], h["ebT8"].rearrange("a f p -> a f p")), sgw, 16)
    k.op("gpsimd", lambda e: e.dma_start(h["esel_b"], h["esel"]), sgw, 16)
    sgw_rdy = sgw.n
    gt_dv = 0
    lg_dv = 0
    for tt in range(NT):
        k.wait("tensor", sgw, sgw_rdy)
        if lg_dv:
            k.wait("tensor", dv, lg_dv)
        mm = 0
        for fc in range(FC):
            mm = k.op("tensor", lambda e, fc=fc, tt=tt: e.matmul(
                h["pbC"][:, 0:8], h["xn2_f"][:, fc, tt * 128:(tt + 1) * 128], h["gw_b"][:, fc, :],
                start=(fc == 0), stop=(fc == FC - 1)))
        k.wait("vector", pe, mm)
        k.selfwait("vector")
        G = h["gtile"][:, tt, :]
        msk, msk2 = h["gmsk"][:, 0, :], h["gmsk"][:, 1, :]
        lg_dv = k.op("vector", lambda e: e.tensor_copy(h["gmsk"][:, 2, :], h["pbC"][:, 0:8]))
        LG = h["gmsk"][:, 2, :]
        k.selfwait("vector")
        k.op("vector", lambda e, LG=LG: e.tensor_reduce(h["gscr"][:, 0:1, 0], LG, axis=AX.X, op=ALU.max))
        k.drain("vector")
        k.op("vector", lambda e, LG=LG: e.tensor_scalar(msk, LG, h["gscr"][:, 0:1, 0], None, op0=ALU.is_ge))
        k.drain("vector")
        k.op("vector", lambda e, LG=LG: e.scalar_tensor_tensor(msk2, msk, -1e30, LG, op0=ALU.mult, op1=ALU.add))
        k.selfwait("vector")
        k.op("vector", lambda e: e.tensor_reduce(h["gscr"][:, 1:2, 0], msk2, axis=AX.X, op=ALU.max))
        k.drain("vector")
        k.op("vector", lambda e: e.tensor_tensor(h["gscr"][:, 2:3, 0], h["gscr"][:, 1:2, 0], h["gscr"][:, 0:1, 0], op=ALU.subtract))
        k.selfwait("vector")
        ex_dv = dv.n
        k.wait("scalar", dv, ex_dv)
        k.selfwait("scalar")
        exa = k.op("scalar", lambda e: e.activation(h["gscr"][:, 0:1, 1], h["gscr"][:, 2:3, 0], AF.Exp))
        k.wait("vector", ac, exa)
        k.selfwait("vector")
        k.op("vector", lambda e: e.tensor_copy(h["gscr"][:, 2:3, 0], h["gscr"][:, 0:1, 1]))
        k.selfwait("vector")
        k.op("vector", lambda e: e.tensor_scalar(h["gscr"][:, 1:2, 1], h["gscr"][:, 2:3, 0], 1.0, None, op0=ALU.add))
        k.selfwait("vector")
        k.op("vector", lambda e: e.reciprocal(h["gscr"][:, 3:4, 0], h["gscr"][:, 1:2, 1]))
        k.drain("vector")
        k.op("vector", lambda e, LG=LG: e.tensor_scalar(msk2, LG, h["gscr"][:, 1:2, 0], None, op0=ALU.is_ge))
        k.selfwait("vector")
        k.op("vector", lambda e: e.tensor_tensor(msk2, msk2, msk, op=ALU.subtract))
        k.drain("vector")
        k.op("vector", lambda e: e.tensor_tensor(h["gscr"][:, 2:3, 0], h["gscr"][:, 2:3, 0], h["gscr"][:, 3:4, 0], op=ALU.mult))
        k.drain("vector")
        k.op("vector", lambda e, G=G: e.tensor_scalar(G, msk, h["gscr"][:, 3:4, 0], None, op0=ALU.mult))
        k.drain("vector")
        gt_dv = k.op("vector", lambda e, G=G: e.scalar_tensor_tensor(G, msk2, h["gscr"][:, 2:3, 0], G, op0=ALU.mult, op1=ALU.add))
        k.wait("tensor", dv, gt_dv)
        tr = k.op("tensor", lambda e, G=G: e.transpose(h["pbT"].bitcast(F32)[:, 0:128][0:8, :] if False else h["pbD"][0:8, 0:128], G, h["ident_f"]))
        k.wait("vector", pe, tr)
        k.selfwait("vector")
        gt_dv = k.op("vector", lambda e, tt=tt: e.tensor_copy(h["GT2"][:, tt * 128:(tt + 1) * 128], h["pbD"][0:8, 0:128]))
    wl_free = [0, 0]
    moe_done_ac = 0
    for half in range(2):
        hs = slice(half * 512, (half + 1) * 512)
        for e_i in range(8):
            k.wait("tensor", dv, gt_dv)
            gx = k.op("tensor", lambda e, e_i=e_i, half=half, hs=hs: e.matmul(
                h["pbC"][:, 0:512], h["esel_b"][:, e_i, :], h["GT2"][:, hs], start=True, stop=True))
            k.wait("vector", pe, gx)
            k.selfwait("vector")
            k.op("vector", lambda e: e.tensor_copy(h["sil_scr"], h["pbC"][:, 0:512]))
            k.selfwait("vector")
            xe_dv = 0
            for fc in range(FC):
                xe_dv = k.op("vector", lambda e, fc=fc, half=half, hs=hs: e.tensor_tensor(
                    h["xe"][:, fc, :], h["xn2_f"][:, fc, hs], h["sil_scr"], op=ALU.mult))
            for oc in range(FC):
                slot = oc % 2
                if wl_free[slot]:
                    k.wait("gpsimd", pe, wl_free[slot])
                wld = k.op("gpsimd", lambda e, e_i=e_i, oc=oc, slot=slot: e.dma_start(
                    h["wl_h"][slot][:, 0:FC, :], h["moeT_stat"][e_i, oc].rearrange("f p n -> p f n")), swl[slot], 16)
                k.wait("tensor", swl[slot], wld)
                if oc == 0:
                    k.wait("tensor", dv, xe_dv)
                bank = h["pbA"] if slot == 0 else h["pbB"]
                mm = 0
                for fc in range(FC):
                    mm = k.op("tensor", lambda e, fc=fc, oc=oc, bank=bank, slot=slot, e_i=e_i: e.matmul(
                        bank[:, :], h["wl_h"][slot][:, fc, :], h["xe"][:, fc, :],
                        start=(fc == 0), stop=(fc == FC - 1 and e_i != 7)))
                if e_i == 7:
                    mm = k.op("tensor", lambda e, oc=oc, half=half, bank=bank, hs=hs: e.matmul(
                        bank[:, :], h["ebT_b"][:, oc, :], h["GT2"][:, hs], start=False, stop=True))
                wl_free[slot] = mm
                k.wait("vector", pe, mm)
                k.selfwait("vector")
                if e_i == 0:
                    k.op("vector", lambda e, oc=oc, bank=bank: e.tensor_copy(h["mo_f"][:, oc, :], bank[:, :]))
                else:
                    k.op("vector", lambda e, oc=oc, bank=bank: e.tensor_tensor(h["mo_f"][:, oc, :], h["mo_f"][:, oc, :], bank[:, :], op=ALU.add))
                k.wait("tensor", dv)
        for oc in range(FC):
            k.selfwait("scalar")
            k.wait("scalar", dv)
            sa = k.op("scalar", lambda e, oc=oc: e.activation(h["sil_scr"], h["mo_f"][:, oc, :], AF.Silu))
            k.wait("vector", ac, sa)
            k.selfwait("vector")
            k.op("vector", lambda e, oc=oc, half=half, hs=hs: e.tensor_tensor(h["prod"][:, oc, :], h["s2"][:, oc, hs], h["sil_scr"], op=ALU.mult))
        prod_dv = dv.n
        for t4 in range(4):
            tt = half * 4 + t4
            slot = t4 % 2
            xb = h["x1t"][slot]
            k.wait("sync", dv)
            k.op("sync", lambda e, tt=tt, xb=xb: e.dma_start(
                xb, h["x1_sp"].rearrange("(n p) d -> n p d", p=128)[tt]), sx1l[slot], 16)
            k.wait("tensor", dv, prod_dv)
            pe_h = []
            for hf in range(2):
                pb = h["pbT"] if hf == 0 else h["pbT2"]
                for fc8 in range(8):
                    oc = hf * 8 + fc8
                    v = k.op("tensor", lambda e, oc=oc, fc8=fc8, pb=pb, t4=t4: e.transpose(
                        pb[:, fc8 * 128:(fc8 + 1) * 128], h["prod"][:, oc, t4 * 128:(t4 + 1) * 128], h["ident_b"]))
                pe_h.append(v)
            k.wait("vector", sx1l[slot])
            out_dv = 0
            for hf in range(2):
                pb = h["pbT"] if hf == 0 else h["pbT2"]
                k.wait("vector", pe)
                k.selfwait("vector")
                k.op("vector", lambda e, hf=hf: e.tensor_copy(h["prcp"][:, hf * 1024:(hf + 1) * 1024], pb if False else (h["pbT"] if hf == 0 else h["pbT2"])))
                k.selfwait("vector")
                out_dv = k.op("vector", lambda e, hf=hf, xb=xb: e.tensor_tensor(
                    h["xnt"][0][:, hf * 1024:(hf + 1) * 1024], xb[:, hf * 1024:(hf + 1) * 1024],
                    h["prcp"][:, hf * 1024:(hf + 1) * 1024], op=ALU.add))
            k.wait("tensor", dv, out_dv)
            k.wait("scalar", dv, out_dv)
            k.op("scalar", lambda e, tt=tt: e.dma_start(
                h["x_out"].rearrange("(n p) d -> n p d", p=128)[tt], h["xnt"][0]), sout, 16)
            k.wait("vector", sout)
    h["l1_done"] = dict(sout=sout.n)


from concourse.bass_utils import run_bass_kernel_spmd


def _wstat(w):
    dout, din = w.shape
    wt = np.ascontiguousarray(w.T)
    a = wt.reshape(din // 128, 128, dout // 128, 128)
    return np.ascontiguousarray(a.transpose(2, 0, 1, 3))


def build_full(dbg=()):
    nc = bass.Bass("TRN2", target_bir_lowering=False)
    k = KB(nc, 0, dbg=dbg)
    h = decl_l0(k)
    decl_l0b(k, h)
    decl_l0_ffn(k, h)
    decl_l0_attn(k, h)
    decl_l1(k, h, standalone=False)
    with nc.Block() as block:
        phase_consts(k, h)
        phase_ln_quant(k, h)
        phase_kv_totals(k, h)
        a_end_pe = k.sem("pe").n
        phase_ffn(k, h, a_end_pe)
        b_end = dict(pe=k.sem("pe").n, ac=k.sem("ac").n, dv=k.sem("dv").n)
        phase_attn(k, h, b_end)
        phase_wo_epilogue(k, h)
        phase_barrier(k, extra_sems=("sx1", "sxt0", "sxt1", "swo", "sbo"))
        import os as _os
        if _os.environ.get("KDBG_X1"):
            o_x1 = k.dout("o_x1", [T, D])
            k.op("scalar", lambda e: e.dma_start(o_x1, h["x1_sp"]), k.sem("wdbg"), 16)
            k.wait("scalar", k.sem("wdbg"))
        phase_ln_quant(k, h, x_src=h["x1_sp"], xn_f_only=True)
        phase_barrier(k)
        if _os.environ.get("KDBG_L1"):
            o_xn2 = k.dout("o_xn2", [128, FC, T], BF16)
            k.wait("scalar", k.sem("dv"))
            k.op("scalar", lambda e: e.dma_start(o_xn2, h["xn_f"]), k.sem("wdbg"), 16)
            k.wait("scalar", k.sem("wdbg"))
        phase_l1_attn(k, h)
        phase_barrier(k)
        if _os.environ.get("KDBG_L1"):
            o_s2d = k.dout("o_s2d", [128, FC, T], BF16)
            k.wait("scalar", k.sem("dv"))
            k.wait("scalar", k.sem("ac"))
            k.op("scalar", lambda e: e.dma_start(o_s2d, h["s2"]), k.sem("wdbg"), 16)
            k.wait("scalar", k.sem("wdbg"))
        phase_l1_moe(k, h)
        if _os.environ.get("KDBG_L1"):
            o_gt2 = k.dout("o_gt2", [8, T], BF16)
            o_mo2 = k.dout("o_mo2", [128, FC, 512], BF16)
            o_pr2 = k.dout("o_pr2", [128, FC, 512], BF16)
            k.wait("scalar", k.sem("dv"))
            k.wait("scalar", k.sem("ac"))
            k.op("scalar", lambda e: e.dma_start(o_gt2, h["GT2"]), k.sem("wdbg"), 16)
            k.op("scalar", lambda e: e.dma_start(o_mo2, h["mo_f"]), k.sem("wdbg"), 16)
            k.op("scalar", lambda e: e.dma_start(o_pr2, h["prod"]), k.sem("wdbg"), 16)
            k.wait("scalar", k.sem("wdbg"))
        k.wait("gpsimd", k.sem("sout"))
        k.wait("gpsimd", k.sem("cc"), h["cc_done"])
        k.emit(block)
    return nc, k


def kernel(**inputs):
    inp = {kk: np.ascontiguousarray(np.asarray(v, dtype=np.float32)) for kk, v in inputs.items()}
    x_full = inp["L0_attn_x"].reshape(8192, 2048)
    shared = {
        "wkT": wT_rhs_tiles(inp["L0_attn_Wk"]),
        "wvT": wT_rhs_tiles(inp["L0_attn_Wv"]),
        "bkv_rows": np.ascontiguousarray(np.stack([inp["L0_attn_bk"], inp["L0_attn_bv"]])),
        "gate_w_t": np.ascontiguousarray(inp["L0_ffn_gate_w"].reshape(64, 128, D)),
        "up_w_t": np.ascontiguousarray(inp["L0_ffn_up_w"].reshape(64, 128, D)),
        "down_w_t": np.ascontiguousarray(inp["L0_ffn_down_w"].reshape(16, 128, IF)),
        "wqT_stat": _wstat(inp["L0_attn_Wq"]),
        "wgT_stat": _wstat(inp["L0_attn_Wg"]),
        "woT": wT_rhs_tiles(inp["L0_attn_Wo"]),
        "bqg_cols": np.ascontiguousarray(np.concatenate(
            [bias_col(inp["L0_attn_bq"]), bias_col(inp["L0_attn_bg"])], 1)),
        "bo_row": np.ascontiguousarray(inp["L0_attn_bo"].reshape(1, D)),
        "winT_stat": _wstat(inp["L1_attn_W_in"]),
        "convT_stat": _wstat(inp["L1_attn_conv_w"]),
        "woutT_stat": _wstat(inp["L1_attn_W_out"]),
        "dcol": bias_col(inp["L1_attn_D"]),
        "gwT": np.ascontiguousarray(inp["L1_ffn_gate_w"].T.reshape(FC, 128, 8)),
        "ebT8": np.ascontiguousarray(inp["L1_ffn_expert_b"].reshape(8, FC, 128)),
        "moeT_stat": np.ascontiguousarray(np.stack([
            inp["L1_ffn_expert_w"][e].reshape(FC, 128, FC, 128).transpose(2, 0, 1, 3)
            for e in range(8)])),
    }
    esel = np.zeros((8, 8, 128), np.float32)
    for e in range(8):
        esel[e, e, :] = 1.0
    shared["esel"] = esel
    pswap = np.zeros((128, 128), np.float32)
    for i in range(128):
        pswap[i, i ^ 1] = 1.0
    shared["pswap"] = pswap
    shared["identf"] = np.eye(128, dtype=np.float32)
    shared["onesf"] = np.ones((128, 512), np.float32)

    in_maps = []
    for c in range(NCORE):
        hc = host_consts(c)
        ct = np.ascontiguousarray(hc["rope_ct"].reshape(NT, 128, 4, 512).transpose(2, 0, 1, 3))
        st = np.ascontiguousarray(hc["rope_st"].reshape(NT, 128, 4, 512).transpose(2, 0, 1, 3))
        wm = np.zeros((128, 8), np.float32)
        for j in range(NCORE):
            if j // 4 == c // 4 and j % 4 < c % 4:
                wm[:, j] = 1.0
        m = dict(shared)
        m.update({
            "x_in": np.ascontiguousarray(x_full[c * T:(c + 1) * T]),
            "ctT": ct, "stT": st,
            "rope_cf": np.ascontiguousarray(hc["rope_ct"][:, :128].T),
            "rope_sf": np.ascontiguousarray(hc["rope_st"][:, :128].T),
            "wmask": wm,
        })
        in_maps.append(m)

    nc, k = build_full()
    res = run_bass_kernel_spmd(nc, in_maps, core_ids=list(range(NCORE)))
    out = np.concatenate([res.results[c]["x_out"] for c in range(NCORE)], 0)
    return np.ascontiguousarray(out.reshape(2, 4096, 2048).astype(np.float32))
